# version 08
# speedup vs baseline: 1.2985x; 1.0235x over previous
import contextlib
import numpy as np
import concourse.bass as bass
import concourse.mybir as mybir
from concourse.bass_utils import run_bass_kernel_spmd

F32 = mybir.dt.float32
BF16 = mybir.dt.bfloat16
I32 = mybir.dt.int32
ALU = mybir.AluOpType
AF = mybir.ActivationFunctionType
AX = mybir.AxisListType.X

NCORES = 8
DM = 4096
S_FULL = 16384
T = S_FULL // NCORES
P = 512
TH = T + P
NPC = DM // 128
CH = 128
NCHUNK = TH // CH
HALOCH = P // CH
NCORECH = NCHUNK - HALOCH
FFI = 2048
NIC = FFI // 128
NBAND = 16
AW = 144
EPS = 1e-5

_CACHE = {}


class SemCnt:
    def __init__(self, nc, es):
        self.nc, self.es, self.h, self.c = nc, es, {}, {}

    def __call__(self, name):
        if name not in self.h:
            self.h[name] = self.es.enter_context(self.nc.semaphore(name))
            self.c[name] = 0
        return self.h[name]

    def inc(self, name, k=1):
        self(name)
        self.c[name] += k
        return self.c[name]

    def val(self, name):
        self(name)
        return self.c[name]


def build_program(debug=(), stage=99, fsteps=None):
    nc = bass.Bass("TRN2", target_bir_lowering=False)

    def din(name, shape, dtype=F32):
        return nc.dram_tensor(name, list(shape), dtype, kind="ExternalInput").ap()

    def scr(name, shape, dtype=BF16):
        return nc.dram_tensor(name, list(shape), dtype, kind="Internal").ap()

    xin = din("xin", (DM, TH + 1))
    wT = {p: din("w%sT" % p, (DM, DM)) for p in "kvrgo"}
    wgateT = din("wgateT", (DM, FFI))
    wupT = din("wupT", (DM, FFI))
    wdownT = din("wdownT", (FFI, DM))
    tmix = din("tmix", (128, 4 * NPC))
    td_r = din("td_r", (128, NPC))
    tf_r = din("tf_r", (128, NPC))
    gnw_r = din("gnw_r", (128, NPC))
    gnb_r = din("gnb_r", (128, NPC))
    invg_r = din("invg_r", (128, NPC))
    invu_r = din("invu_r", (128, NPC))
    invi_r = din("invi_r", (128, NIC))
    swg_r = din("swg_r", (128, NIC))
    swu_r = din("swu_r", (128, NIC))
    swd_bc_in = din("swd_bc_in", (128, DM))
    eye_in = din("eye_in", (128, 128))
    selm_in = din("selm_in", (128, 128), mybir.dt.uint8)
    bstrip_in = din("bstrip_in", (128, NBAND * 32))
    bb2_in = din("bb2_in", (2, 128))
    bs2_in = din("bs2_in", (128, 2))

    out = nc.dram_tensor("out", [T, DM], F32, kind="ExternalOutput").ap()

    mixbuf = scr("mixbuf", (DM, TH))
    kfm = scr("kfm", (DM, TH))
    vfm = scr("vfm", (DM, TH))
    rfm = scr("rfm", (DM, T + NBAND))
    gfm = scr("gfm", (DM, T))
    ofm = scr("ofm", (DM, T))
    yfm = scr("yfm", (DM, T))
    zprefm = scr("zprefm", (DM, T))
    z1fm = scr("z1fm", (DM, T))
    img = scr("img", (NCORECH * 128, 64 * AW))
    dbga2 = nc.dram_tensor("dbg_a2", [NPC * 128, 256], BF16,
                           kind="ExternalOutput").ap() if "a2" in debug else None

    dbg_map = {"mixbuf": mixbuf, "kfm": kfm, "vfm": vfm, "rfm": rfm, "gfm": gfm,
               "ofm": ofm, "yfm": yfm, "zprefm": zprefm, "z1fm": z1fm, "img": img}
    dbg = []
    for name in debug:
        if name == "a2":
            continue
        src = dbg_map[name]
        dst = nc.dram_tensor("dbg_" + name, list(src.tensor.shape), BF16,
                             kind="ExternalOutput").ap()
        dbg.append((src, dst))

    es = contextlib.ExitStack()
    S = SemCnt(nc, es)
    uid = [0]
    with es:
        sbt = lambda n, s, d=BF16: es.enter_context(nc.sbuf_tensor(n, list(s), d))
        ptt = lambda n, s, d=F32: es.enter_context(nc.psum_tensor(n, list(s), d))

        NCC = 8
        C_U, C_W32, C_W128, C_TF, C_GNW, C_GNB, C_IVG, C_IVU = range(NCC)
        consts = sbt("consts", (128, NCC * NPC), F32)
        mixc = sbt("mixc", (128, 4 * NPC), F32)
        ffc = sbt("ffc", (128, 3 * NIC), F32)
        eyeb = sbt("eyeb", (128, 128), BF16)
        eyef = sbt("eyef", (128, 128), F32)
        selm = sbt("selm", (128, 128), mybir.dt.uint8)
        bstr = sbt("bstr", (128, NBAND * 32), BF16)
        bs2 = sbt("bs2", (128, 2), BF16)
        bb2 = sbt("bb2", (2, 128), BF16)
        swdbc = sbt("swdbc", (128, DM), BF16)
        onesb = sbt("onesb", (128, 128), BF16)
        onesf = sbt("onesf", (128, 128), F32)
        bb2f = sbt("bb2f", (2, 128), F32)
        state = sbt("state", (128, NPC * 64), F32)
        zpad = sbt("zpad", (128, 512), BF16)
        scan_scope = contextlib.ExitStack()
        sct = lambda n, s, d=BF16: scan_scope.enter_context(
            nc.sbuf_tensor(n, list(s), d))
        patQ32 = sct("patQ32", (128, NPC * 128), BF16)
        patQ16 = sct("patQ16", (128, NPC * 128), BF16)
        pat128 = sct("pat128", (128, NPC * 128), BF16)
        bandw = sct("bandw", (128, NPC * NBAND), F32)

        psA = ptt("psA", (128, 512))
        psB = ptt("psB", (128, 512))
        psC = ptt("psC", (128, 512))
        psD = ptt("psD", (128, 512))
        psE = ptt("psE", (128, 512))
        psF = ptt("psF", (128, 512))
        ptA = ptt("ptA", (128, 1024), BF16)
        ptB = ptt("ptB", (128, 1024), BF16)

        def slab_sb(t, x):
            return bass.AP(t, 0, [[NPC * x, 128], [x, NPC], [1, x]])

        def slab_hbm(dram, col0, x, npc=NPC):
            w = dram.tensor.shape[1]
            return bass.AP(dram.tensor, col0, [[w, 128], [128 * w, npc], [1, x]])

        def pcview(patt, pc, w=128):
            return bass.AP(patt, pc * 128, [[patt.shape[1], 128], [1, w]])

        def rev32(patt, pc):
            return bass.AP(patt, pc * 128 + 31,
                           [[patt.shape[1], 128], [32, 4], [-1, 32]])

        def rev16(patt, pc):
            return bass.AP(patt, pc * 128 + 15,
                           [[patt.shape[1], 128], [16, 8], [-1, 16]])

        def rev128(patt, pc):
            return bass.AP(patt, pc * 128 + 127,
                           [[patt.shape[1], 128], [-1, 128]])

        io = sct("io", (128, 3 * 128), F32)
        with nc.Block() as blk:
            @blk.sync
            def _(s):
                for dstt, srcin in (
                        (consts[:, C_U * NPC:(C_U + 1) * NPC], td_r),
                        (consts[:, C_TF * NPC:(C_TF + 1) * NPC], tf_r),
                        (consts[:, C_GNW * NPC:(C_GNW + 1) * NPC], gnw_r),
                        (consts[:, C_GNB * NPC:(C_GNB + 1) * NPC], gnb_r),
                        (consts[:, C_IVG * NPC:(C_IVG + 1) * NPC], invg_r),
                        (consts[:, C_IVU * NPC:(C_IVU + 1) * NPC], invu_r),
                        (mixc[:], tmix),
                        (ffc[:, 0:NIC], invi_r),
                        (ffc[:, NIC:2 * NIC], swg_r),
                        (ffc[:, 2 * NIC:3 * NIC], swu_r),
                        (eyef[:], eye_in),
                        (bb2f[:], bb2_in)):
                    s.dma_start(out=dstt, in_=srcin).then_inc(S("sdl"), 16)

            @blk.gpsimd
            def _(g):
                for dstt, srcin in ((eyeb, eye_in), (selm, selm_in),
                                    (bstr, bstrip_in), (bs2, bs2_in),
                                    (bb2, bb2_in), (swdbc, swd_bc_in)):
                    g.dma_start(out=dstt[:], in_=srcin).then_inc(S("sdg"), 16)
                g.iota(io[:, 0:128], [[0, 4], [1, 32]], channel_multiplier=0,
                       allow_small_or_imprecise_dtypes=True)
                g.iota(io[:, 128:256], [[0, 8], [1, 16]], channel_multiplier=0,
                       allow_small_or_imprecise_dtypes=True)
                g.iota(io[:, 256:384], [[1, 128]], channel_multiplier=0,
                       allow_small_or_imprecise_dtypes=True)
                g.drain().then_inc(S("iod"), 1)

            @blk.scalar
            def _(s):
                s.wait_ge(S("sdl"), 12 * 16)
                cu = consts[:, C_U * NPC:(C_U + 1) * NPC]
                s.activation(cu, cu, AF.Exp)
                s.drain()
                s.mul(cu, cu, -1.0)
                s.drain()
                s.activation(consts[:, C_W32 * NPC:(C_W32 + 1) * NPC], cu,
                             AF.Exp, bias=0.0, scale=32.0)
                s.activation(consts[:, C_W128 * NPC:(C_W128 + 1) * NPC], cu,
                             AF.Exp, bias=0.0, scale=128.0)
                s.drain().then_inc(S("ud"), 1)

            @blk.vector
            def _(v):
                v.wait_ge(S("ud"), 1)
                v.wait_ge(S("iod"), 1)
                for pc in range(NPC):
                    ucol = consts[:, C_U * NPC + pc:C_U * NPC + pc + 1]
                    v.tensor_scalar(patQ32[:, pc * 128:(pc + 1) * 128],
                                    io[:, 0:128], ucol, None, ALU.mult)
                    v.tensor_scalar(patQ16[:, pc * 128:(pc + 1) * 128],
                                    io[:, 128:256], ucol, None, ALU.mult)
                    v.tensor_scalar(pat128[:, pc * 128:(pc + 1) * 128],
                                    io[:, 256:384], ucol, None, ALU.mult)
                v.memset(state[:], 0.0)
                v.memset(zpad[:], 0.0)
                v.memset(onesb[:], 1.0)
                v.memset(onesf[:], 1.0)
                v.drain().then_inc(S("pm"), 1)

            @blk.scalar
            def _(s):
                s.wait_ge(S("pm"), 1)
                for patt in (patQ32, patQ16, pat128):
                    s.activation(patt[:], patt[:], AF.Exp)
                s.drain().then_inc(S("pe"), 1)

            @blk.vector
            def _(v):
                v.wait_ge(S("pe"), 1)
                v.wait_ge(S("sdg"), 96)
                for pc in range(NPC):
                    v.tensor_copy(bandw[:, pc * NBAND:pc * NBAND + 1],
                                  consts[:, C_TF * NPC + pc:C_TF * NPC + pc + 1])
                    v.tensor_copy(bandw[:, pc * NBAND + 1:(pc + 1) * NBAND],
                                  pat128[:, pc * 128:pc * 128 + NBAND - 1])
                v.drain().then_inc(S("sdone"), 1)

            @blk.scalar
            def _(s):
                s.wait_ge(S("sdone"), 1)
                s.dma_start(out=rfm[:, T:T + NBAND],
                            in_=bass.AP(zpad, 0, [[512, 128], [0, NPC], [1, NBAND]])
                            ).then_inc(S("zp"), 16)
                nimg = NCORECH * 128 // 128
                for r in range(nimg):
                    s.dma_start(
                        out=bass.AP(img.tensor, r * 128 * 64 * AW,
                                    [[64 * AW, 128], [512, 64 * AW // 512], [1, 512]]),
                        in_=bass.AP(zpad, 0, [[512, 128], [0, 64 * AW // 512], [1, 512]]),
                    ).then_inc(S("zp"), 16)
                s.wait_ge(S("zp"), 16 * (1 + nimg))
        for n, k in (("sdl", 192), ("sdg", 96), ("iod", 1), ("ud", 1), ("pm", 1),
                     ("pe", 1), ("sdone", 1), ("zp", 16 * (1 + NCORECH))):
            S.inc(n, k)

        def mix_phase(p_idx, span, s0, dstt):
            nit = span // 512
            b_ld, b_cp, b_st = S.val("mxld"), S.val("mxcp"), S.val("mxst")
            with contextlib.ExitStack() as ph:
                uid[0] += 1
                _u = uid[0]
                psb = lambda n, s, d=BF16: ph.enter_context(
                    nc.sbuf_tensor("u%d_%s" % (_u, n), list(s), d))
                mx_x = [psb("mxx%d" % i, (128, NPC * 513)) for i in range(2)]
                mx_o = [psb("mxo%d" % i, (128, NPC * 512)) for i in range(2)]
                with nc.Block() as blk:
                    @blk.gpsimd
                    def _(g):
                        for it in range(nit):
                            if it >= 2:
                                g.wait_ge(S("mxcp"), b_cp + it - 1)
                            g.dma_start(
                                out=bass.AP(mx_x[it % 2], 0,
                                            [[NPC * 513, 128], [513, NPC], [1, 513]]),
                                in_=slab_hbm(xin, s0 + it * 512, 513),
                            ).then_inc(S("mxld"), 16)

                    @blk.vector
                    def _(v):
                        for it in range(nit):
                            v.wait_ge(S("mxld"), b_ld + 16 * (it + 1))
                            if it >= 2:
                                v.wait_ge(S("mxst"), b_st + 16 * (it - 1))
                            xt, ot = mx_x[it % 2], mx_o[it % 2]
                            for pc in range(NPC):
                                v.tensor_tensor(ot[:, pc * 512:(pc + 1) * 512],
                                                xt[:, pc * 513 + 1:pc * 513 + 513],
                                                xt[:, pc * 513:pc * 513 + 512],
                                                ALU.subtract)
                            v.drain()
                            for pc in range(NPC):
                                mcol = mixc[:, p_idx * NPC + pc:p_idx * NPC + pc + 1]
                                d = ot[:, pc * 512:(pc + 1) * 512]
                                v.scalar_tensor_tensor(
                                    d, d, mcol, xt[:, pc * 513:pc * 513 + 512],
                                    ALU.mult, ALU.add)
                            v.drain().then_inc(S("mxcp"), 1)

                    @blk.scalar
                    def _(s):
                        for it in range(nit):
                            s.wait_ge(S("mxcp"), b_cp + it + 1)
                            s.dma_start(
                                out=slab_hbm(dstt, it * 512, 512),
                                in_=bass.AP(mx_o[it % 2], 0,
                                            [[NPC * 512, 128], [512, NPC], [1, 512]]),
                            ).then_inc(S("mxst"), 16)
                        s.wait_ge(S("mxst"), b_st + 16 * nit)
            S.inc("mxld", 16 * nit)
            S.inc("mxcp", nit)
            S.inc("mxst", 16 * nit)

        def proj_phase(wsrc, msrc, dstt, span, residual=False):
            ncols = wsrc.tensor.shape[1]
            nccb = ncols // 1024
            nit = span // 512
            nhalf = nccb * 2
            ngcc = nccb * nit * 8
            b_wld, b_wfr = S.val("wld"), S.val("wfr")
            b_mld, b_mfr = S.val("mld"), S.val("mfr")
            b_mm, b_ev, b_st = S.val("pmm"), S.val("pev"), S.val("pst")
            b_xl = S.val("pxl")
            with contextlib.ExitStack() as ph:
                uid[0] += 1
                _u = uid[0]
                psb = lambda n, s, d=BF16: ph.enter_context(
                    nc.sbuf_tensor("u%d_%s" % (_u, n), list(s), d))
                wsb = [psb("pw%d" % i, (128, NPC * 512)) for i in range(2)]
                mvt = [psb("pv%d" % i, (128, NPC * 512)) for i in range(2)]
                ot = [psb("po%d" % i, (128, 512)) for i in range(2)]
                xt = ([psb("px%d" % i, (128, 512), F32) for i in range(2)]
                      if residual else None)
                with nc.Block() as blk:
                    @blk.gpsimd
                    def _(g):
                        for hf in range(nhalf):
                            if hf >= 2:
                                g.wait_ge(S("wfr"), b_wfr + hf - 1)
                            g.dma_start(
                                out=slab_sb(wsb[hf % 2], 512),
                                in_=slab_hbm(wsrc, hf * 512, 512),
                            ).then_inc(S("wld"), 16)

                    @blk.sync
                    def _(s):
                        n = 0
                        for ccb in range(nccb):
                            for it in range(nit):
                                if n >= 2:
                                    s.wait_ge(S("mfr"), b_mfr + n - 1)
                                s.dma_start(
                                    out=slab_sb(mvt[n % 2], 512),
                                    in_=slab_hbm(msrc, it * 512, 512),
                                ).then_inc(S("mld"), 16)
                                if residual:
                                    for cc in range(8):
                                        gcc = n * 8 + cc
                                        if gcc >= 2:
                                            s.wait_ge(S("pev"), b_ev + gcc - 1)
                                        row0 = ccb * 1024 + cc * 128
                                        s.dma_start(
                                            out=xt[gcc % 2][:],
                                            in_=bass.AP(xin.tensor,
                                                        row0 * (TH + 1) + P + 1 + it * 512,
                                                        [[TH + 1, 128], [1, 512]]),
                                        ).then_inc(S("pxl"), 16)
                                n += 1

                    @blk.tensor
                    def _(t):
                        n = 0
                        gcc = 0
                        for ccb in range(nccb):
                            for it in range(nit):
                                t.wait_ge(S("mld"), b_mld + 16 * (n + 1))
                                for cc in range(8):
                                    hf = ccb * 2 + cc // 4
                                    if cc in (0, 4) and it == 0:
                                        t.wait_ge(S("wld"), b_wld + 16 * (hf + 1))
                                    bank = psA if (gcc % 2 == 0) else psB
                                    if gcc >= 2:
                                        t.wait_ge(S("pev"), b_ev + gcc - 1)
                                    wcol = (cc % 4) * 128
                                    for fc in range(NPC):
                                        t.matmul(bank[:, 0:512],
                                                 wsb[hf % 2][:, fc * 512 + wcol:
                                                             fc * 512 + wcol + 128],
                                                 mvt[n % 2][:, fc * 512:(fc + 1) * 512],
                                                 start=(fc == 0), stop=(fc == NPC - 1))
                                    t.drain().then_inc(S("pmm"), 1)
                                    if it == nit - 1 and cc in (3, 7):
                                        t.nop().then_inc(S("wfr"), 1)
                                    gcc += 1
                                t.nop().then_inc(S("mfr"), 1)
                                n += 1

                    @blk.vector
                    def _(v):
                        gcc = 0
                        for ccb in range(nccb):
                            for it in range(nit):
                                for cc in range(8):
                                    v.wait_ge(S("pmm"), b_mm + gcc + 1)
                                    if gcc >= 2:
                                        v.wait_ge(S("pst"), b_st + 16 * (gcc - 1))
                                    bank = psA if (gcc % 2 == 0) else psB
                                    if residual:
                                        v.wait_ge(S("pxl"), b_xl + 16 * (gcc + 1))
                                        v.tensor_tensor(ot[gcc % 2][:], bank[:, 0:512],
                                                        xt[gcc % 2][:], ALU.add)
                                    else:
                                        v.tensor_copy(ot[gcc % 2][:], bank[:, 0:512])
                                    v.drain().then_inc(S("pev"), 1)
                                    gcc += 1

                    @blk.scalar
                    def _(s):
                        gcc = 0
                        w = dstt.tensor.shape[1]
                        for ccb in range(nccb):
                            for it in range(nit):
                                for cc in range(8):
                                    s.wait_ge(S("pev"), b_ev + gcc + 1)
                                    row0 = ccb * 1024 + cc * 128
                                    s.dma_start(
                                        out=bass.AP(dstt.tensor, row0 * w + it * 512,
                                                    [[w, 128], [1, 512]]),
                                        in_=ot[gcc % 2][:],
                                    ).then_inc(S("pst"), 16)
                                    gcc += 1
                        s.wait_ge(S("pst"), b_st + 16 * ngcc)
            S.inc("wld", 16 * nhalf)
            S.inc("wfr", nccb * 2)
            S.inc("mld", 16 * nccb * nit)
            S.inc("mfr", nccb * nit)
            S.inc("pmm", ngcc)
            S.inc("pev", ngcc)
            S.inc("pst", 16 * ngcc)
            if residual:
                S.inc("pxl", 16 * ngcc)

        if stage >= 1:
            mix_phase(0, TH, 0, mixbuf)
            proj_phase(wT["k"], mixbuf, kfm, TH)
            mix_phase(1, TH, 0, mixbuf)
            proj_phase(wT["v"], mixbuf, vfm, TH)
            mix_phase(2, T, P, mixbuf)
            proj_phase(wT["r"], mixbuf, rfm, T)
            mix_phase(3, T, P, mixbuf)
            proj_phase(wT["g"], mixbuf, gfm, T)

        if stage >= 2:
            b_ld = S.val("bld")
            b_pv, b_pt, b_mm2 = S.val("bpv"), S.val("bpt"), S.val("bmm")
            b_tp, b_sc = S.val("btp"), S.val("bsc")
            with contextlib.ExitStack() as ph:
                uid[0] += 1
                _u = uid[0]
                psb = lambda n, s, d=BF16: ph.enter_context(
                    nc.sbuf_tensor("u%d_%s" % (_u, n), list(s), d))
                kb = [psb("kb%d" % i, (128, T)) for i in range(2)]
                rb = [psb("rb%d" % i, (128, T + NBAND)) for i in range(2)]
                pbuf = [psb("pb%d" % i, (128, T)) for i in range(2)]
                ssb = [psb("ssb%d" % i, (32, T)) for i in range(2)]
                sst = [psb("sst%d" % i, (32, T)) for i in range(2)]
                bands = [psA, psB, psC, psD]
                with nc.Block() as blk:
                    @blk.sync
                    def _(s):
                        for pc in range(NPC):
                            if pc >= 2:
                                s.wait_ge(S("bpv"), b_pv + (pc - 1) * NBAND)
                            s.dma_start(out=kb[pc % 2][:],
                                        in_=bass.AP(kfm.tensor, pc * 128 * TH + P,
                                                    [[TH, 128], [1, T]])
                                        ).then_inc(S("bld"), 16)
                            s.dma_start(out=rb[pc % 2][:],
                                        in_=bass.AP(rfm.tensor,
                                                    pc * 128 * (T + NBAND),
                                                    [[T + NBAND, 128], [1, T + NBAND]])
                                        ).then_inc(S("bld"), 16)

                    @blk.vector
                    def _(v):
                        for pc in range(NPC):
                            v.wait_ge(S("bld"), b_ld + 32 * (pc + 1))
                            for dlt in range(NBAND):
                                gi = pc * NBAND + dlt
                                if gi >= 2:
                                    v.wait_ge(S("bpt"), b_pt + gi - 1)
                                wcol = bandw[:, pc * NBAND + dlt:pc * NBAND + dlt + 1]
                                v.scalar_tensor_tensor(
                                    pbuf[dlt % 2][:], kb[pc % 2][:], wcol,
                                    rb[pc % 2][:, dlt:dlt + T], ALU.mult, ALU.mult)
                                v.drain().then_inc(S("bpv"), 1)
                            v.wait_ge(S("bmm"), b_mm2 + pc + 1)
                            if pc >= 2:
                                v.wait_ge(S("bsc"), b_sc + 16 * 32 * (pc - 1))
                            for tt in range(4):
                                v.tensor_copy(ssb[pc % 2][:, tt * 512:(tt + 1) * 512],
                                              bands[tt][0:32, 0:512])
                            v.drain()
                            v.transpose(sst[pc % 2][:], ssb[pc % 2][:])
                            v.drain().then_inc(S("btp"), 1)

                    @blk.tensor
                    def _(t):
                        for pc in range(NPC):
                            for dlt in range(NBAND):
                                gi = pc * NBAND + dlt
                                t.wait_ge(S("bpv"), b_pv + gi + 1)
                                if dlt == 0 and pc >= 1:
                                    t.wait_ge(S("btp"), b_tp + pc)
                                for tt in range(4):
                                    t.matmul(bands[tt][0:32, 0:512],
                                             bstr[:, dlt * 32:(dlt + 1) * 32],
                                             pbuf[dlt % 2][:, tt * 512:(tt + 1) * 512],
                                             start=(dlt == 0), stop=(dlt == NBAND - 1))
                                t.nop().then_inc(S("bpt"), 1)
                            t.drain().then_inc(S("bmm"), 1)

                    @blk.gpsimd
                    def _(g):
                        for pc in range(NPC):
                            g.wait_ge(S("btp"), b_tp + pc + 1)
                            for tt in range(4):
                                for ghi in range(4):
                                    for hh in range(2):
                                        sp = bass.AP(
                                            sst[pc % 2],
                                            tt * 512 + ghi * 128 + hh * 16,
                                            [[T, 32], [32, 4], [1, 16]])
                                        dp = bass.AP(
                                            img.tensor,
                                            (tt * 4 + ghi) * 128 * 64 * AW
                                            + (2 * pc + hh) * AW,
                                            [[64 * AW + 1, 32],
                                             [32 * (64 * AW + 1), 4], [1, 16]])
                                        g.dma_start(out=dp, in_=sp
                                                    ).then_inc(S("bsc"), 16)
                        g.wait_ge(S("bsc"), b_sc + 16 * 32 * NPC)
            S.inc("bld", 32 * NPC)
            S.inc("bpv", NPC * NBAND)
            S.inc("bpt", NPC * NBAND)
            S.inc("bmm", NPC)
            S.inc("btp", NPC)
            S.inc("bsc", 16 * 32 * NPC)

        if stage >= 3:
            NG = NCHUNK * NPC
            b_ld, b_ai = S.val("cld"), S.val("cai")
            b_vA, b_tB, b_vC, b_tD, b_vE = (S.val("vA"), S.val("tB"), S.val("vC"),
                                            S.val("tD"), S.val("vE"))
            b_ost = S.val("ost")
            BL32 = [(1, 0, 0), (2, 1, 0), (3, 2, 0), (2, 0, 1), (3, 1, 1), (3, 0, 2)]
            BL16 = [(1, 0), (3, 2), (5, 4), (7, 6)]
            with contextlib.ExitStack() as ph:
                uid[0] += 1
                _u = uid[0]
                psb = lambda n, s, d=BF16: ph.enter_context(
                    nc.sbuf_tensor("u%d_%s" % (_u, n), list(s), d))
                a2dump = psb("a2dump", (128, NPC * 256)) if dbga2 is not None else None
                kcs = [psb("kcs%d" % i, (128, NPC * 128)) for i in range(2)]
                vcs = [psb("vcs%d" % i, (128, NPC * 128)) for i in range(2)]
                rcs = [psb("rcs%d" % i, (128, NPC * 128)) for i in range(2)]
                aim = [psb("aim%d" % i, (128, 64 * AW)) for i in range(2)]
                q32v = [psb("q32v%d" % i, (128, 128)) for i in range(2)]
                q16v = [psb("q16v%d" % i, (128, 128)) for i in range(2)]
                r128v = [psb("r128v%d" % i, (128, 128)) for i in range(2)]
                kd0v = [psb("kd0v%d" % i, (128, 128)) for i in range(2)]
                kd16v = [psb("kd16v%d" % i, (128, 128)) for i in range(2)]
                kdx = [psb("kdx%d" % i, (128, 96)) for i in range(2)]
                k128v = [psb("k128v%d" % i, (128, 128)) for i in range(2)]
                scast = [psb("scast%d" % i, (128, 64)) for i in range(2)]
                ktm = [psb("ktm%d" % i, (128, 128)) for i in range(2)]
                vtm = [psb("vtm%d" % i, (128, 128)) for i in range(2)]
                a2 = [psb("a2_%d" % i, (128, 256)) for i in range(2)]
                obuf = [psb("obuf%d" % i, (128, 128)) for i in range(2)]
                with nc.Block() as blk:
                    @blk.sync
                    def _(s):
                        for ch in range(NCHUNK):
                            core = ch >= HALOCH
                            tb = ch * 128
                            if ch >= 2:
                                s.wait_ge(S("vE"), b_vE + (ch - 1) * NPC)
                            s.dma_start(out=slab_sb(kcs[ch % 2], 128),
                                        in_=slab_hbm(kfm, tb, 128)
                                        ).then_inc(S("cld"), 16)
                            s.dma_start(out=slab_sb(vcs[ch % 2], 128),
                                        in_=slab_hbm(vfm, tb, 128)
                                        ).then_inc(S("cld"), 16)
                            if core:
                                s.dma_start(out=slab_sb(rcs[ch % 2], 128),
                                            in_=slab_hbm(rfm, tb - P, 128)
                                            ).then_inc(S("cld"), 16)
                                s.dma_start(
                                    out=aim[ch % 2][:],
                                    in_=bass.AP(img.tensor,
                                                (ch - HALOCH) * 128 * 64 * AW,
                                                [[64 * AW, 128], [1, 64 * AW]]),
                                ).then_inc(S("cai"), 16)

                    @blk.vector
                    def _(v):
                        def emitA(g):
                            ch, pc = divmod(g, NPC)
                            core = ch >= HALOCH
                            nld = sum(48 if c >= HALOCH else 32
                                      for c in range(ch + 1))
                            v.wait_ge(S("cld"), b_ld + nld)
                            if g >= 2:
                                v.wait_ge(S("tB"), b_tB + g - 1)
                            kc = kcs[ch % 2][:, pc * 128:(pc + 1) * 128]
                            v.tensor_tensor(k128v[g % 2][:], kc,
                                            rev128(pat128, pc), ALU.mult)
                            if core:
                                rc = rcs[ch % 2][:, pc * 128:(pc + 1) * 128]
                                v.tensor_tensor(q32v[g % 2][:], rc,
                                                pcview(patQ32, pc), ALU.mult)
                                v.tensor_tensor(q16v[g % 2][:], rc,
                                                pcview(patQ16, pc), ALU.mult)
                                v.tensor_tensor(r128v[g % 2][:], rc,
                                                pcview(pat128, pc), ALU.mult)
                                v.tensor_tensor(kd0v[g % 2][:], kc,
                                                rev32(patQ32, pc), ALU.mult)
                                v.tensor_tensor(kd16v[g % 2][:], kc,
                                                rev16(patQ16, pc), ALU.mult)
                                v.tensor_scalar(scast[g % 2][:],
                                                state[:, pc * 64:(pc + 1) * 64],
                                                1.0, None, ALU.mult)
                                v.drain()
                                w32c = consts[:, C_W32 * NPC + pc:
                                              C_W32 * NPC + pc + 1]
                                v.tensor_scalar(kdx[g % 2][:, 0:64],
                                                kd0v[g % 2][:, 0:64],
                                                w32c, None, ALU.mult)
                                v.drain()
                                v.tensor_scalar(kdx[g % 2][:, 64:96],
                                                kdx[g % 2][:, 0:32],
                                                w32c, None, ALU.mult)
                            v.drain().then_inc(S("vA"), 1)

                        def emitC(g):
                            ch, pc = divmod(g, NPC)
                            core = ch >= HALOCH
                            nai = 16 * max(0, ch - HALOCH + 1)
                            v.wait_ge(S("tB"), b_tB + g + 1)
                            if g >= 2:
                                v.wait_ge(S("tD"), b_tD + g - 1)
                            ptX = ptA if (g % 2 == 0) else ptB
                            v.tensor_copy(ktm[g % 2][:], ptX[:, 0:128])
                            v.tensor_copy(vtm[g % 2][:], ptX[:, 128:256])
                            if core:
                                v.wait_ge(S("cai"), b_ai + nai)
                                bank = psA if (g % 2 == 0) else psB
                                for hh in range(2):
                                    v.select(
                                        a2[g % 2][:, hh * 128:(hh + 1) * 128],
                                        selm[:], bank[:, hh * 128:hh * 128 + 128],
                                        aim[ch % 2][:, (2 * pc + hh) * AW:
                                                     (2 * pc + hh) * AW + 128])
                                if dbga2 is not None and ch == HALOCH:
                                    v.drain()
                                    v.tensor_copy(
                                        a2dump[:, pc * 256:(pc + 1) * 256],
                                        a2[g % 2][:])
                            v.drain().then_inc(S("vC"), 1)

                        def emitE(g):
                            ch, pc = divmod(g, NPC)
                            core = ch >= HALOCH
                            v.wait_ge(S("tD"), b_tD + g + 1)
                            psS = psE if (g % 2 == 0) else psF
                            w128c = consts[:, C_W128 * NPC + pc:
                                           C_W128 * NPC + pc + 1]
                            v.scalar_tensor_tensor(
                                state[:, pc * 64:(pc + 1) * 64],
                                state[:, pc * 64:(pc + 1) * 64], w128c,
                                psS[:, 0:64], ALU.mult, ALU.add)
                            if core:
                                gq = (ch - HALOCH) * NPC + pc
                                psO = psC if (g % 2 == 0) else psD
                                if gq >= 2:
                                    v.wait_ge(S("ost"), b_ost + 16 * (gq - 1))
                                v.tensor_copy(obuf[gq % 2][:], psO[:, 0:128])
                            v.drain().then_inc(S("vE"), 1)

                        for g in range(NG):
                            emitA(g)
                            if g >= 1:
                                emitC(g - 1)
                            if g >= 2:
                                emitE(g - 2)
                        emitC(NG - 1)
                        emitE(NG - 2)
                        emitE(NG - 1)

                    @blk.tensor
                    def _(t):
                        def emitB(g):
                            ch, pc = divmod(g, NPC)
                            core = ch >= HALOCH
                            t.wait_ge(S("vA"), b_vA + g + 1)
                            if g >= 2:
                                t.wait_ge(S("vC"), b_vC + g - 1)
                            ptX = ptA if (g % 2 == 0) else ptB
                            t.transpose(ptX[:, 0:128], k128v[g % 2][:], eyeb[:])
                            t.transpose(ptX[:, 128:256],
                                        vcs[ch % 2][:, pc * 128:(pc + 1) * 128],
                                        eyeb[:])
                            if core:
                                bank = psA if (g % 2 == 0) else psB
                                psO = psC if (g % 2 == 0) else psD
                                if g >= 2:
                                    t.wait_ge(S("vE"), b_vE + g - 1)
                                for hh in range(2):
                                    hr = hh * 64
                                    co = hh * 128
                                    for (bi, bj, depth) in BL32:
                                        if depth == 0:
                                            stat = kd0v[g % 2][hr:hr + 64,
                                                               bj * 32:bj * 32 + 32]
                                        elif depth == 1:
                                            stat = kdx[g % 2][hr:hr + 64,
                                                              bj * 32:bj * 32 + 32]
                                        else:
                                            stat = kdx[g % 2][hr:hr + 64, 64:96]
                                        t.matmul(bank[bj * 32:bj * 32 + 32,
                                                      co + bi * 32:co + bi * 32 + 32],
                                                 stat,
                                                 q32v[g % 2][hr:hr + 64,
                                                             bi * 32:bi * 32 + 32],
                                                 start=True, stop=True,
                                                 tile_position=(hr, bj * 32))
                                    for (bi16, bj16) in BL16:
                                        t.matmul(
                                            bank[bj16 * 16:bj16 * 16 + 16,
                                                 co + bi16 * 16:co + bi16 * 16 + 16],
                                            kd16v[g % 2][hr:hr + 64,
                                                         bj16 * 16:bj16 * 16 + 16],
                                            q16v[g % 2][hr:hr + 64,
                                                        bi16 * 16:bi16 * 16 + 16],
                                            start=True, stop=True,
                                            tile_position=(hr, bj16 * 16 // 32 * 32))
                                    t.matmul(psO[hr:hr + 64, 0:128],
                                             scast[g % 2][hr:hr + 64, :],
                                             r128v[g % 2][hr:hr + 64, :],
                                             start=True, stop=False,
                                             tile_position=(hr, hr))
                            t.drain().then_inc(S("tB"), 1)

                        def emitD(g):
                            ch, pc = divmod(g, NPC)
                            core = ch >= HALOCH
                            t.wait_ge(S("vC"), b_vC + g + 1)
                            if g >= 2:
                                t.wait_ge(S("vE"), b_vE + g - 1)
                            psS = psE if (g % 2 == 0) else psF
                            for hh in range(2):
                                hr = hh * 64
                                if core:
                                    psO = psC if (g % 2 == 0) else psD
                                    t.matmul(psO[hr:hr + 64, 0:128],
                                             vtm[g % 2][:, hr:hr + 64],
                                             a2[g % 2][:, hh * 128:(hh + 1) * 128],
                                             start=False, stop=True,
                                             tile_position=(0, hr))
                                t.matmul(psS[hr:hr + 64, 0:64],
                                         ktm[g % 2][:, hr:hr + 64],
                                         vtm[g % 2][:, hr:hr + 64],
                                         start=True, stop=True,
                                         tile_position=(0, hr))
                            t.drain().then_inc(S("tD"), 1)

                        for g in range(NG):
                            emitB(g)
                            if g >= 1:
                                emitD(g - 1)
                        emitD(NG - 1)

                    @blk.scalar
                    def _(s):
                        for ch in range(HALOCH, NCHUNK):
                            tb = ch * 128
                            for pc in range(NPC):
                                g = ch * NPC + pc
                                gq = (ch - HALOCH) * NPC + pc
                                s.wait_ge(S("vE"), b_vE + g + 1)
                                s.dma_start(
                                    out=bass.AP(ofm.tensor, pc * 128 * T + tb - P,
                                                [[T, 128], [1, 128]]),
                                    in_=obuf[gq % 2][:],
                                ).then_inc(S("ost"), 16)
                        if dbga2 is not None:
                            for pc in range(NPC):
                                s.dma_start(
                                    out=bass.AP(dbga2.tensor, pc * 128 * 256,
                                                [[256, 128], [1, 256]]),
                                    in_=a2dump[:, pc * 256:(pc + 1) * 256],
                                ).then_inc(S("ost"), 16)
                        s.wait_ge(S("ost"), b_ost + 16 * NCORECH * NPC
                                  + (16 * NPC if dbga2 is not None else 0))
            S.inc("cld", 16 * (2 * NCHUNK + NCORECH))
            S.inc("cai", 16 * NCORECH)
            for n in ("vA", "tB", "vC", "tD", "vE"):
                S.inc(n, NG)
            S.inc("ost", 16 * NCORECH * NPC)

        scan_scope.close()

        if stage >= 4:
            NITG = NPC * (T // 512)
            bs = {k: S.val(k) for k in ("gld", "gs1", "gt1", "gv1", "gs2", "gt2",
                                        "gv2", "gst", "gvb")}
            with contextlib.ExitStack() as ph:
                uid[0] += 1
                _u = uid[0]
                psb = lambda n, s, d=BF16: ph.enter_context(
                    nc.sbuf_tensor("u%d_%s" % (_u, n), list(s), d))
                otl = [psb("gno%d" % i, (128, 512)) for i in range(2)]
                gtl = [psb("gng%d" % i, (128, 512)) for i in range(2)]
                sqt = [psb("gnsq%d" % i, (128, 512)) for i in range(2)]
                mus = [psb("gnmu%d" % i, (2, 512), F32) for i in range(2)]
                msq = [psb("gnm2%d" % i, (2, 512), F32) for i in range(2)]
                var = [psb("gnvr%d" % i, (2, 512), F32) for i in range(2)]
                rst = [psb("gnrs%d" % i, (2, 512), F32) for i in range(2)]
                musb = [psb("gnmub%d" % i, (2, 512)) for i in range(2)]
                rstb = [psb("gnrsb%d" % i, (2, 512)) for i in range(2)]
                ytl = [psb("gny%d" % i, (128, 512)) for i in range(2)]
                with nc.Block() as blk:
                    @blk.sync
                    def _(s):
                        n = 0
                        for pc in range(NPC):
                            for it in range(T // 512):
                                if n >= 2:
                                    s.wait_ge(S("gv2"), bs["gv2"] + n - 1)
                                s.dma_start(out=otl[n % 2][:],
                                            in_=bass.AP(ofm.tensor,
                                                        pc * 128 * T + it * 512,
                                                        [[T, 128], [1, 512]])
                                            ).then_inc(S("gld"), 16)
                                s.dma_start(out=gtl[n % 2][:],
                                            in_=bass.AP(gfm.tensor,
                                                        pc * 128 * T + it * 512,
                                                        [[T, 128], [1, 512]])
                                            ).then_inc(S("gld"), 16)
                                n += 1

                    @blk.scalar
                    def _(s):
                        n = 0
                        for pc in range(NPC):
                            for it in range(T // 512):
                                s.wait_ge(S("gld"), bs["gld"] + 32 * (n + 1))
                                if n >= 2:
                                    s.wait_ge(S("gt1"), bs["gt1"] + n - 1)
                                s.activation(sqt[n % 2][:], otl[n % 2][:], AF.Square)
                                s.activation(gtl[n % 2][:], gtl[n % 2][:], AF.Silu)
                                s.drain().then_inc(S("gs1"), 1)
                                s.wait_ge(S("gv1"), bs["gv1"] + n + 1)
                                if n >= 2:
                                    s.wait_ge(S("gt2"), bs["gt2"] + n - 1)
                                s.activation(rst[n % 2][:], var[n % 2][:], AF.Ln)
                                s.drain()
                                s.activation(rst[n % 2][:], rst[n % 2][:], AF.Exp,
                                             bias=0.0, scale=-0.5)
                                s.drain().then_inc(S("gs2"), 1)
                                n += 1

                    @blk.tensor
                    def _(t):
                        n = 0
                        for pc in range(NPC):
                            for it in range(T // 512):
                                t.wait_ge(S("gs1"), bs["gs1"] + n + 1)
                                if n >= 2:
                                    t.wait_ge(S("gv1"), bs["gv1"] + n - 1)
                                pst = psE if (n % 2 == 0) else psF
                                t.matmul(pst[0:2, 0:512], bs2[:], otl[n % 2][:],
                                         start=True, stop=True)
                                t.matmul(pst[32:34, 0:512], bs2[:], sqt[n % 2][:],
                                         start=True, stop=True, tile_position=(0, 32))
                                t.drain().then_inc(S("gt1"), 1)
                                t.wait_ge(S("gvb"), bs["gvb"] + n + 1)
                                if n >= 2:
                                    t.wait_ge(S("gv2"), bs["gv2"] + n - 1)
                                bmu = psC if (n % 2 == 0) else psD
                                brs = psA if (n % 2 == 0) else psB
                                t.matmul(bmu[:, 0:512], bb2[:], musb[n % 2][:],
                                         start=True, stop=True)
                                t.matmul(brs[:, 0:512], bb2[:], rstb[n % 2][:],
                                         start=True, stop=True)
                                t.drain().then_inc(S("gt2"), 1)
                                n += 1

                    @blk.vector
                    def _(v):
                        n = 0
                        for pc in range(NPC):
                            for it in range(T // 512):
                                v.wait_ge(S("gt1"), bs["gt1"] + n + 1)
                                pst = psE if (n % 2 == 0) else psF
                                v.tensor_scalar(mus[n % 2][:], pst[0:2, 0:512],
                                                1.0 / 64, None, ALU.mult)
                                v.tensor_scalar(msq[n % 2][:], pst[32:34, 0:512],
                                                1.0 / 64, None, ALU.mult)
                                v.drain()
                                v.tensor_tensor(var[n % 2][:], mus[n % 2][:],
                                                mus[n % 2][:], ALU.mult)
                                v.drain()
                                v.tensor_tensor(var[n % 2][:], msq[n % 2][:],
                                                var[n % 2][:], ALU.subtract)
                                v.drain()
                                v.tensor_scalar(var[n % 2][:], var[n % 2][:], EPS,
                                                None, ALU.add)
                                v.tensor_copy(musb[n % 2][:], mus[n % 2][:])
                                v.drain().then_inc(S("gv1"), 1)
                                v.wait_ge(S("gs2"), bs["gs2"] + n + 1)
                                v.tensor_copy(rstb[n % 2][:], rst[n % 2][:])
                                v.drain().then_inc(S("gvb"), 1)
                                v.wait_ge(S("gt2"), bs["gt2"] + n + 1)
                                if n >= 2:
                                    v.wait_ge(S("gst"), bs["gst"] + 16 * (n - 1))
                                bmu = psC if (n % 2 == 0) else psD
                                brs = psA if (n % 2 == 0) else psB
                                gw = consts[:, C_GNW * NPC + pc:C_GNW * NPC + pc + 1]
                                gb = consts[:, C_GNB * NPC + pc:C_GNB * NPC + pc + 1]
                                y = ytl[n % 2]
                                v.tensor_tensor(y[:], otl[n % 2][:], bmu[:, 0:512],
                                                ALU.subtract)
                                v.drain()
                                v.tensor_tensor(y[:], y[:], brs[:, 0:512], ALU.mult)
                                v.drain()
                                v.tensor_scalar(y[:], y[:], gw, gb, ALU.mult, ALU.add)
                                v.drain()
                                v.tensor_tensor(y[:], y[:], gtl[n % 2][:], ALU.mult)
                                v.drain().then_inc(S("gv2"), 1)
                                n += 1

                    @blk.gpsimd
                    def _(g):
                        n = 0
                        for pc in range(NPC):
                            for it in range(T // 512):
                                g.wait_ge(S("gv2"), bs["gv2"] + n + 1)
                                g.dma_start(
                                    out=bass.AP(yfm.tensor, pc * 128 * T + it * 512,
                                                [[T, 128], [1, 512]]),
                                    in_=ytl[n % 2][:],
                                ).then_inc(S("gst"), 16)
                                n += 1
                        g.wait_ge(S("gst"), bs["gst"] + 16 * NITG)
            S.inc("gld", 32 * NITG)
            for n in ("gs1", "gt1", "gv1", "gs2", "gt2", "gv2", "gvb"):
                S.inc(n, NITG)
            S.inc("gst", 16 * NITG)

        if stage >= 5:
            proj_phase(wT["o"], yfm, zprefm, T, residual=True)

        if stage >= 6:
            bs = {k: S.val(k) for k in ("lld", "lsq", "lmm", "lv1", "lsr", "lv2",
                                        "lbc", "lap", "lst")}
            with contextlib.ExitStack() as ph:
                uid[0] += 1
                _u = uid[0]
                psb = lambda n, s, d=BF16: ph.enter_context(
                    nc.sbuf_tensor("u%d_%s" % (_u, n), list(s), d))
                zsl = psb("l1z", (128, NPC * 512))
                z1s = psb("l1o", (128, NPC * 512))
                sq = [psb("l1sq%d" % i, (128, 512)) for i in range(2)]
                mus = psb("l1mu", (1, 512), F32)
                msq = psb("l1m2", (1, 512), F32)
                var = psb("l1vr", (1, 512), F32)
                rst = psb("l1rs", (1, 512), F32)
                with nc.Block() as blk:
                    @blk.sync
                    def _(s):
                        for it in range(T // 512):
                            if it >= 1:
                                s.wait_ge(S("lap"), bs["lap"] + it * NPC)
                            s.dma_start(out=slab_sb(zsl, 512),
                                        in_=slab_hbm(zprefm, it * 512, 512)
                                        ).then_inc(S("lld"), 16)

                    @blk.scalar
                    def _(s):
                        for it in range(T // 512):
                            s.wait_ge(S("lld"), bs["lld"] + 16 * (it + 1))
                            for cc in range(NPC):
                                idx = it * NPC + cc
                                if idx >= 2:
                                    it2, cc2 = divmod(idx - 2, NPC)
                                    s.wait_ge(S("lmm"),
                                              bs["lmm"] + it2 * (NPC + 1) + cc2 + 1)
                                s.activation(sq[cc % 2][:],
                                             zsl[:, cc * 512:(cc + 1) * 512],
                                             AF.Square)
                                s.drain().then_inc(S("lsq"), 1)
                            s.wait_ge(S("lv1"), bs["lv1"] + it + 1)
                            s.activation(rst[:], var[:], AF.Ln)
                            s.drain()
                            s.activation(rst[:], rst[:], AF.Exp, bias=0.0,
                                         scale=-0.5)
                            s.drain().then_inc(S("lsr"), 1)

                    @blk.tensor
                    def _(t):
                        for it in range(T // 512):
                            for cc in range(NPC):
                                t.wait_ge(S("lsq"), bs["lsq"] + it * NPC + cc + 1)
                                if cc == 0 and it >= 1:
                                    t.wait_ge(S("lv1"), bs["lv1"] + it)
                                t.matmul(psE[0:1, 0:512], onesb[:, 0:1],
                                         zsl[:, cc * 512:(cc + 1) * 512],
                                         start=(cc == 0), stop=(cc == NPC - 1))
                                t.matmul(psF[0:1, 0:512], onesb[:, 0:1],
                                         sq[cc % 2][:],
                                         start=(cc == 0), stop=(cc == NPC - 1))
                                t.nop().then_inc(S("lmm"), 1)
                            t.drain().then_inc(S("lmm"), 1)
                            t.wait_ge(S("lv2"), bs["lv2"] + it + 1)
                            if it >= 1:
                                t.wait_ge(S("lap"), bs["lap"] + it * NPC)
                            t.matmul(psC[:, 0:512], onesf[0:1, :], mus[:],
                                     start=True, stop=True)
                            t.matmul(psD[:, 0:512], onesf[0:1, :], rst[:],
                                     start=True, stop=True)
                            t.drain().then_inc(S("lbc"), 1)

                    @blk.vector
                    def _(v):
                        for it in range(T // 512):
                            v.wait_ge(S("lmm"), bs["lmm"] + (it + 1) * (NPC + 1))
                            v.tensor_scalar(mus[:], psE[0:1, 0:512], 1.0 / DM,
                                            None, ALU.mult)
                            v.tensor_scalar(msq[:], psF[0:1, 0:512], 1.0 / DM,
                                            None, ALU.mult)
                            v.drain()
                            v.tensor_tensor(var[:], mus[:], mus[:], ALU.mult)
                            v.drain()
                            v.tensor_tensor(var[:], msq[:], var[:], ALU.subtract)
                            v.drain()
                            v.tensor_scalar(var[:], var[:], EPS, None, ALU.add)
                            v.drain().then_inc(S("lv1"), 1)
                            v.wait_ge(S("lsr"), bs["lsr"] + it + 1)
                            v.drain().then_inc(S("lv2"), 1)
                            v.wait_ge(S("lbc"), bs["lbc"] + it + 1)
                            for cc in range(NPC):
                                if it * NPC + cc >= 2:
                                    v.wait_ge(S("lst"), bs["lst"] +
                                              16 * (it * NPC + cc - 1))
                                zc = zsl[:, cc * 512:(cc + 1) * 512]
                                oc = z1s[:, cc * 512:(cc + 1) * 512]
                                v.tensor_tensor(oc, zc, psC[:, 0:512], ALU.subtract)
                                v.drain()
                                v.tensor_tensor(oc, oc, psD[:, 0:512], ALU.mult)
                                v.drain().then_inc(S("lap"), 1)

                    @blk.gpsimd
                    def _(g):
                        for it in range(T // 512):
                            for cc in range(NPC):
                                g.wait_ge(S("lap"), bs["lap"] + it * NPC + cc + 1)
                                g.dma_start(
                                    out=bass.AP(z1fm.tensor, cc * 128 * T + it * 512,
                                                [[T, 128], [1, 512]]),
                                    in_=z1s[:, cc * 512:(cc + 1) * 512],
                                ).then_inc(S("lst"), 16)
                        g.wait_ge(S("lst"), bs["lst"] + 16 * (T // 512) * NPC)
            S.inc("lld", 16 * (T // 512))
            S.inc("lsq", (T // 512) * NPC)
            S.inc("lmm", (T // 512) * (NPC + 1))
            S.inc("lv1", T // 512)
            S.inc("lsr", T // 512)
            S.inc("lv2", T // 512)
            S.inc("lbc", T // 512)
            S.inc("lap", (T // 512) * NPC)
            S.inc("lst", 16 * (T // 512) * NPC)

        if stage >= 7:
            with contextlib.ExitStack() as ph:
                uid[0] += 1
                _u = uid[0]
                psb = lambda n, s, d=BF16: ph.enter_context(
                    nc.sbuf_tensor("u%d_%s" % (_u, n), list(s), d))
                WAR = 57344
                farena = psb("farena", (128, WAR))
                C_ZSL, C_QG, C_QU = 0, 16384, 32768
                C_WGA, C_WUA = 49152, 53248
                C_WGB, C_WUB = 8192, 12288
                C_Q2 = 0
                C_WD = [16384, 24576]
                C_Z2 = 32768

                def av(c0, w):
                    return bass.AP(farena, c0, [[WAR, 128], [1, w]])

                def avslab(c0, x, npc=NPC):
                    return bass.AP(farena, c0, [[WAR, 128], [x, npc], [1, x]])

                z1T = psb("z1T", (128, 4 * DM))
                hsl = psb("hsl", (128, NIC * 512))
                maxb = psb("maxb", (128, 512), F32)
                rmx = psb("rmx", (128, 8), F32)
                scv = psb("scv", (128, 4), F32)
                siv = psb("siv", (128, 4), F32)
                sIv = psb("sIv", (128, 4), F32)
                spT = psb("spT", (1, 1024), F32)
                sgb = psb("sgb", (128, 512))
                sub2 = psb("sub2", (128, 512))
                q0f = psb("q0f", (128, 512), F32)
                qi32 = psb("qi32", (128, 512), I32)
                tgv = psb("tgv", (128, 512), F32)
                tgs = psb("tgs", (128, 512))
                tuv = psb("tuv", (128, 512), F32)
                sq512 = psb("sq512", (128, 512))
                ssum = psb("ssum", (128, 32), F32)
                ssq = psb("ssq", (128, 32), F32)
                stat2 = psb("stat2", (128, 8), F32)
                mucol = psb("mucol", (128, 4), F32)
                rstds = psb("rstds", (128, 4), F32)
                oto = psb("oto", (128, 512), F32)
                oto2 = [psb("oto2_%d" % i, (128, 512), F32) for i in range(2)]
                negc = psb("negc", (128, 2 * NPC + NIC), F32)

                ENGS = ("s", "g", "v", "sc", "t")
                esem = {e: S("ff_" + e) for e in ENGS}
                ebase = {e: S.val("ff_" + e) for e in ENGS}
                ecnt = {e: 0 for e in ENGS}
                steps = []

                class FS:
                    __slots__ = ("eng", "fn", "deps", "eidx")

                def st(eng, fn, deps=()):
                    s_ = FS()
                    s_.eng, s_.fn = eng, fn
                    s_.deps = [d for d in deps if d is not None]
                    s_.eidx = ecnt[eng]
                    ecnt[eng] += 1
                    steps.append(s_)
                    return s_

                dcnt = {"fds": S.val("fds"), "fdg": S.val("fdg"),
                        "fdsc": S.val("fdsc")}
                fdsc_base = dcnt["fdsc"]
                dtot = {"fds": 0, "fdg": 0, "fdsc": 0}

                def quant(src_c0, src_t, nch, inv_ap, ninv_ap, qdst_c0, scb_dst,
                          sI_dst, pre_deps, tp_extra, tp2_extra, bc_extra):
                    def srcv(cc):
                        if src_t is None:
                            return av(src_c0 + cc * 512, 512)
                        return src_t[:, cc * 512:(cc + 1) * 512]

                    def _mx(v):
                        v.memset(maxb[:], 0.0)
                        v.drain()
                        for cc in range(nch):
                            v.scalar_tensor_tensor(maxb[:], srcv(cc),
                                                   inv_ap[:, cc:cc + 1],
                                                   maxb[:], ALU.mult, ALU.max)
                            v.drain()
                            v.scalar_tensor_tensor(maxb[:], srcv(cc),
                                                   ninv_ap[:, cc:cc + 1],
                                                   maxb[:], ALU.mult, ALU.max)
                            v.drain()
                    h_mx = st("v", _mx, pre_deps)

                    def _tp(t):
                        for gg in range(4):
                            t.transpose(psE[:, gg * 128:(gg + 1) * 128],
                                        maxb[:, gg * 128:(gg + 1) * 128], eyef[:])
                    h_tp = st("t", _tp, [h_mx] + tp_extra)

                    def _rm(v):
                        for gg in range(4):
                            v.reduce_max(rmx[:, gg:gg + 1],
                                         psE[:, gg * 128:(gg + 1) * 128], axis=AX)
                        v.drain()
                        v.tensor_scalar(scv[:], rmx[:, 0:4], 1.0 / 127.0, 1e-8,
                                        ALU.mult, ALU.max)
                        v.drain()
                        v.reciprocal(siv[:], scv[:])
                        if sI_dst is not None:
                            v.tensor_copy(sI_dst[:], scv[:])
                        v.drain()
                    h_rm = st("v", _rm, [h_tp])

                    def _tp2(t):
                        for gg in range(4):
                            t.transpose(psF[0:1, gg * 128:(gg + 1) * 128],
                                        siv[:, gg:gg + 1], eyef[:])
                            t.transpose(psE[0:1, gg * 128:(gg + 1) * 128],
                                        scv[:, gg:gg + 1], eyef[:])
                    h_tp2 = st("t", _tp2, [h_rm] + tp2_extra)

                    def _ev(v):
                        v.tensor_copy(spT[0:1, 0:512], psF[0:1, 0:512])
                        v.tensor_copy(spT[0:1, 512:1024], psE[0:1, 0:512])
                        v.drain()
                    h_ev = st("v", _ev, [h_tp2])

                    def _bc(t):
                        for gg in range(4):
                            t.matmul(psC[:, gg * 128:(gg + 1) * 128],
                                     onesf[0:1, :],
                                     spT[0:1, gg * 128:(gg + 1) * 128],
                                     start=True, stop=True)
                            t.matmul(psD[:, gg * 128:(gg + 1) * 128],
                                     onesf[0:1, :],
                                     spT[0:1, 512 + gg * 128:512 + (gg + 1) * 128],
                                     start=True, stop=True)
                    h_bc = st("t", _bc, [h_ev] + bc_extra)

                    def _q(v):
                        if scb_dst is not None:
                            v.tensor_copy(scb_dst[:], psD[:, 0:512])
                        for cc in range(nch):
                            v.scalar_tensor_tensor(q0f[:], srcv(cc),
                                                   inv_ap[:, cc:cc + 1],
                                                   psC[:, 0:512], ALU.mult, ALU.mult)
                            v.drain()
                            v.tensor_copy(qi32[:], q0f[:])
                            v.drain()
                            v.tensor_copy(av(qdst_c0 + cc * 512, 512), qi32[:])
                            v.drain()
                    h_q = st("v", _q, [h_bc])
                    return {"mx": h_mx, "rm": h_rm, "bc": h_bc, "q": h_q,
                            "tp": h_tp, "tp2": h_tp2, "ev": h_ev}

                def _neg(v):
                    v.tensor_scalar(negc[:, 0:NPC],
                                    consts[:, C_IVG * NPC:(C_IVG + 1) * NPC],
                                    -1.0, None, ALU.mult)
                    v.tensor_scalar(negc[:, NPC:2 * NPC],
                                    consts[:, C_IVU * NPC:(C_IVU + 1) * NPC],
                                    -1.0, None, ALU.mult)
                    v.tensor_scalar(negc[:, 2 * NPC:2 * NPC + NIC], ffc[:, 0:NIC],
                                    -1.0, None, ALU.mult)
                    v.drain()
                st("v", _neg, [])

                prev = {"qUq": None, "ztlast": None, "dmmlast": None,
                        "aplast": None, "mm15": None, "interq": None}

                for it in range(4):
                    def _ld(s, it=it):
                        s.dma_start(out=avslab(C_ZSL, 512),
                                    in_=slab_hbm(z1fm, it * 512, 512)
                                    ).then_inc(S("fds"), 16)
                        dcnt["fds"] += 16
                        s.wait_ge(S("fds"), dcnt["fds"])
                    h_ld = st("s", _ld, [prev["qUq"], prev["ztlast"],
                                         prev["dmmlast"], prev["mm15"]])
                    dtot["fds"] += 16

                    pi = prev["interq"]
                    qG = quant(C_ZSL, None, NPC,
                               consts[:, C_IVG * NPC:(C_IVG + 1) * NPC],
                               negc[:, 0:NPC],
                               C_QG, sgb, None,
                               [h_ld, prev["dmmlast"]],
                               [pi["ev"]] if pi else [],
                               [pi["ev"]] if pi else [],
                               [pi["q"]] if pi else [])
                    qU = quant(C_ZSL, None, NPC,
                               consts[:, C_IVU * NPC:(C_IVU + 1) * NPC],
                               negc[:, NPC:2 * NPC],
                               C_QU, sub2, None,
                               [h_ld, prev["aplast"], qG["q"]],
                               [qG["ev"]], [qG["ev"]], [qG["q"]])

                    h_zt = [None] * 16
                    h_ze = [None] * 16
                    for grp in range(16):
                        tb4, pcg = grp // 4, grp % 4

                        def _zt(t, tb4=tb4, pcg=pcg, grp=grp):
                            ptX = ptA if grp % 2 == 0 else ptB
                            for k in range(8):
                                pc = pcg * 8 + k
                                t.transpose(ptX[:, k * 128:(k + 1) * 128],
                                            av(C_ZSL + pc * 512 + tb4 * 128, 128),
                                            eyeb[:])
                        h_zt[grp] = st("t", _zt,
                                       [h_ld, h_ze[grp - 2] if grp >= 2 else None])

                        def _ze(s, tb4=tb4, pcg=pcg, grp=grp):
                            ptX = ptA if grp % 2 == 0 else ptB
                            s.copy(z1T[:, tb4 * DM + pcg * 1024:
                                       tb4 * DM + (pcg + 1) * 1024],
                                   ptX[:, 0:1024])
                        h_ze[grp] = st("sc", _ze, [h_zt[grp]])

                    h_wl = [None] * NIC
                    h_mm = [None] * NIC
                    h_gv = [None] * NIC
                    h_sl = [None] * NIC
                    h_uv = [None] * NIC
                    for ic in range(NIC):
                        wgc = C_WGA if ic % 2 == 0 else C_WGB
                        wuc = C_WUA if ic % 2 == 0 else C_WUB

                        def _wl(g, ic=ic, wgc=wgc, wuc=wuc):
                            g.dma_start(out=avslab(wgc, 128),
                                        in_=slab_hbm(wgateT, ic * 128, 128)
                                        ).then_inc(S("fdg"), 16)
                            g.dma_start(out=avslab(wuc, 128),
                                        in_=slab_hbm(wupT, ic * 128, 128)
                                        ).then_inc(S("fdg"), 16)
                            dcnt["fdg"] += 32
                            g.wait_ge(S("fdg"), dcnt["fdg"])
                        deps = [h_mm[ic - 2] if ic >= 2 else None]
                        if ic % 2 == 1 and ic == 1:
                            deps += [h_zt[15], qU["q"]]
                        h_wl[ic] = st("g", _wl, deps)
                        dtot["fdg"] += 32

                        bg = psA if ic % 2 == 0 else psE
                        bu = psB if ic % 2 == 0 else psF

                        def _mm(t, ic=ic, wgc=wgc, wuc=wuc, bg=bg, bu=bu):
                            for fc in range(NPC):
                                t.matmul(bg[:, 0:512], av(wgc + fc * 128, 128),
                                         av(C_QG + fc * 512, 512),
                                         start=(fc == 0), stop=(fc == NPC - 1))
                            for fc in range(NPC):
                                t.matmul(bu[:, 0:512], av(wuc + fc * 128, 128),
                                         av(C_QU + fc * 512, 512),
                                         start=(fc == 0), stop=(fc == NPC - 1))
                        h_mm[ic] = st("t", _mm,
                                      [h_wl[ic], qG["q"], qU["q"],
                                       h_gv[ic - 2] if ic >= 2 else None,
                                       h_uv[ic - 2] if ic >= 2 else None])

                        def _gv(v, ic=ic, bg=bg):
                            v.scalar_tensor_tensor(tgv[:], bg[:, 0:512],
                                                   ffc[:, NIC + ic:NIC + ic + 1],
                                                   sgb[:], ALU.mult, ALU.mult)
                            v.drain()
                        h_gv[ic] = st("v", _gv,
                                      [h_mm[ic],
                                       h_sl[ic - 1] if ic >= 1 else None])

                        def _sl(s, ic=ic):
                            s.activation(tgs[:], tgv[:], AF.Silu)
                        h_sl[ic] = st("sc", _sl,
                                      [h_gv[ic],
                                       h_uv[ic - 1] if ic >= 1 else None])

                        def _uv(v, ic=ic, bu=bu):
                            v.scalar_tensor_tensor(tuv[:], bu[:, 0:512],
                                                   ffc[:, 2 * NIC + ic:2 * NIC + ic + 1],
                                                   sub2[:], ALU.mult, ALU.mult)
                            v.drain()
                            v.tensor_tensor(tuv[:], tgs[:], tuv[:], ALU.mult)
                            v.drain()
                            v.tensor_scalar(hsl[:, ic * 512:(ic + 1) * 512], tuv[:],
                                            10.0, -10.0, ALU.min, ALU.max)
                            v.drain()
                        h_uv[ic] = st("v", _uv, [h_sl[ic]])

                    qI = quant(None, hsl, NIC, ffc[:, 0:NIC],
                               negc[:, 2 * NPC:2 * NPC + NIC], C_Q2, None, sIv,
                               [h_uv[NIC - 1]], [h_gv[NIC - 1]], [h_uv[NIC - 1]],
                               [qU["q"]])

                    h_dmm = {}
                    h_dev = {}
                    h_dsq = {}
                    h_dsr = {}
                    lin = 0
                    for cb in range(8):
                        def _wdl(g, cb=cb):
                            g.dma_start(out=avslab(C_WD[cb % 2], 512, npc=NIC),
                                        in_=slab_hbm(wdownT, cb * 512, 512, npc=NIC)
                                        ).then_inc(S("fdg"), 16)
                            dcnt["fdg"] += 16
                            g.wait_ge(S("fdg"), dcnt["fdg"])
                        wdeps = [h_mm[NIC - 1]] if cb < 2 else \
                            [h_dmm[(cb - 2, 3)]]
                        h_wdl = st("g", _wdl, wdeps)
                        dtot["fdg"] += 16
                        for tb4 in range(4):
                            bank = psA if lin % 2 == 0 else psB

                            def _dmm(t, cb=cb, tb4=tb4, bank=bank):
                                for ic in range(NIC):
                                    t.matmul(bank[:, 0:512],
                                             av(C_Q2 + ic * 512 + tb4 * 128, 128),
                                             av(C_WD[cb % 2] + ic * 512, 512),
                                             start=(ic == 0), stop=(ic == NIC - 1))
                            h_dmm[(cb, tb4)] = st(
                                "t", _dmm,
                                [h_wdl, qI["q"],
                                 h_dev.get((cb, tb4 - 2)) or h_dev.get((cb - 1, tb4 + 2))])

                            def _dev(v, cb=cb, tb4=tb4, bank=bank):
                                v.scalar_tensor_tensor(
                                    oto[:], bank[:, 0:512], sIv[:, tb4:tb4 + 1],
                                    swdbc[:, cb * 512:(cb + 1) * 512],
                                    ALU.mult, ALU.mult)
                                v.drain()
                                z2c = av(C_Z2 + tb4 * DM + cb * 512, 512)
                                v.tensor_tensor(z2c, oto[:],
                                                z1T[:, tb4 * DM + cb * 512:
                                                    tb4 * DM + (cb + 1) * 512],
                                                ALU.add)
                                v.drain()
                                v.reduce_sum(ssum[:, tb4 * 8 + cb:tb4 * 8 + cb + 1],
                                             z2c, axis=AX)
                                v.drain()
                            h_dev[(cb, tb4)] = st("v", _dev,
                                                  [h_dmm[(cb, tb4)], h_ze[15],
                                                   qI["rm"]])

                            def _dsq(s, cb=cb, tb4=tb4):
                                s.activation(sq512[:],
                                             av(C_Z2 + tb4 * DM + cb * 512, 512),
                                             AF.Square)
                            h_dsq[(cb, tb4)] = st("sc", _dsq,
                                                  [h_dev[(cb, tb4)]])

                            def _dsr(v, cb=cb, tb4=tb4):
                                v.reduce_sum(ssq[:, tb4 * 8 + cb:tb4 * 8 + cb + 1],
                                             sq512[:], axis=AX)
                                v.drain()
                            h_dsr[(cb, tb4)] = st("v", _dsr, [h_dsq[(cb, tb4)]])
                            lin += 1

                    apidx = [0]
                    h_ap_last = None
                    for tb4 in range(4):
                        def _l1(v, tb4=tb4):
                            v.reduce_sum(stat2[:, 0:1], ssum[:, tb4 * 8:tb4 * 8 + 8],
                                         axis=AX)
                            v.reduce_sum(stat2[:, 1:2], ssq[:, tb4 * 8:tb4 * 8 + 8],
                                         axis=AX)
                            v.drain()
                            v.tensor_scalar(mucol[:, tb4:tb4 + 1], stat2[:, 0:1],
                                            1.0 / DM, None, ALU.mult)
                            v.tensor_scalar(stat2[:, 2:3], stat2[:, 1:2],
                                            1.0 / DM, None, ALU.mult)
                            v.drain()
                            v.tensor_tensor(stat2[:, 3:4], mucol[:, tb4:tb4 + 1],
                                            mucol[:, tb4:tb4 + 1], ALU.mult)
                            v.drain()
                            v.tensor_tensor(stat2[:, 4:5], stat2[:, 2:3],
                                            stat2[:, 3:4], ALU.subtract)
                            v.drain()
                            v.tensor_scalar(stat2[:, 5:6], stat2[:, 4:5], EPS,
                                            None, ALU.add)
                            v.drain()
                        h_l1 = st("v", _l1, [h_dsr[(7, tb4)]])

                        def _l2(s, tb4=tb4):
                            s.activation(stat2[:, 6:7], stat2[:, 5:6], AF.Ln)
                            s.drain()
                            s.activation(stat2[:, 7:8], stat2[:, 6:7], AF.Exp,
                                         bias=0.0, scale=-0.5)
                        h_l2 = st("sc", _l2, [h_l1])

                        def _l3(v, tb4=tb4):
                            v.tensor_copy(rstds[:, tb4:tb4 + 1], stat2[:, 7:8])
                            v.drain()
                        h_l3 = st("v", _l3, [h_l2])

                        for cb in range(8):
                            ai = apidx[0]
                            gai = it * 32 + ai

                            def _ap(v, tb4=tb4, cb=cb, ai=ai, gai=gai):
                                if gai >= 2:
                                    v.wait_ge(S("fdsc"),
                                              fdsc_base + 16 * (gai - 1))
                                v.tensor_scalar(oto2[ai % 2][:],
                                                av(C_Z2 + tb4 * DM + cb * 512, 512),
                                                mucol[:, tb4:tb4 + 1],
                                                rstds[:, tb4:tb4 + 1],
                                                ALU.subtract, ALU.mult)
                                v.drain()
                            h_ap = st("v", _ap, [h_l3])

                            def _st(s, tb4=tb4, cb=cb, it=it, ai=ai):
                                s.dma_start(
                                    out=bass.AP(out.tensor,
                                                (it * 512 + tb4 * 128) * DM + cb * 512,
                                                [[DM, 128], [1, 512]]),
                                    in_=oto2[ai % 2][:],
                                ).then_inc(S("fdsc"), 16)
                                dcnt["fdsc"] += 16
                            h_st = st("sc", _st, [h_ap])
                            dtot["fdsc"] += 16
                            h_ap_last = h_ap
                            apidx[0] += 1

                    prev = {"qUq": qU["q"], "ztlast": h_zt[15],
                            "dmmlast": h_dmm[(7, 3)], "aplast": h_ap_last,
                            "mm15": h_mm[NIC - 1], "interq": qI}

                def _fin(s):
                    s.wait_ge(S("fdsc"), dcnt["fdsc"])
                st("sc", _fin, [])

                with nc.Block() as blk:
                    def emit_engine(eng, e):
                        for s_ in steps:
                            if s_.eng != eng:
                                continue
                            for d in s_.deps:
                                if d.eng != eng:
                                    e.wait_ge(esem[d.eng],
                                              ebase[d.eng] + d.eidx + 1)
                            s_.fn(e)
                            if eng == "t":
                                e.drain().then_inc(esem[eng], 1)
                            elif eng in ("v", "sc"):
                                e.drain().then_inc(esem[eng], 1)
                            else:
                                e.nop().then_inc(esem[eng], 1)

                    @blk.sync
                    def _(s):
                        emit_engine("s", s)

                    @blk.gpsimd
                    def _(g):
                        emit_engine("g", g)

                    @blk.vector
                    def _(v):
                        emit_engine("v", v)

                    @blk.scalar
                    def _(s):
                        emit_engine("sc", s)

                    @blk.tensor
                    def _(t):
                        emit_engine("t", t)
                for e in ENGS:
                    S.inc("ff_" + e, ecnt[e])
                S.inc("fds", dtot["fds"])
                S.inc("fdg", dtot["fdg"])
                S.inc("fdsc", dtot["fdsc"])

        if dbg:
            with nc.Block() as blk:
                b_dg = S.val("dbgc")
                @blk.sync
                def _(s):
                    n = 0
                    for src, dstt in dbg:
                        sh = src.tensor.shape
                        s.dma_start(out=bass.AP(dstt.tensor, 0,
                                                [[sh[1], sh[0]], [1, sh[1]]]),
                                    in_=bass.AP(src.tensor, 0,
                                                [[sh[1], sh[0]], [1, sh[1]]]),
                                    ).then_inc(S("dbgc"), 16)
                        n += 1
                    s.wait_ge(S("dbgc"), b_dg + 16 * n)
            S.inc("dbgc", 16 * len(dbg))

    return nc


def _mk_consts():
    eye = np.eye(128, dtype=np.float32)
    selmask = np.zeros((128, 128), np.float32)
    for j in range(128):
        for i in range(128):
            if (i >> 4) > (j >> 4):
                selmask[j, i] = 1.0
    bstrip = np.zeros((128, NBAND * 32), np.float32)
    for dlt in range(NBAND):
        for c in range(128):
            bstrip[c, dlt * 32 + (c // 64) * 16 + dlt] = 1.0
    bb2 = np.zeros((2, 128), np.float32)
    bb2[0, 0:64] = 1.0
    bb2[1, 64:128] = 1.0
    bs2 = np.zeros((128, 2), np.float32)
    bs2[0:64, 0] = 1.0
    bs2[64:128, 1] = 1.0
    return eye, selmask, bstrip, bb2, bs2


def _resh(v, n=128):
    v = np.asarray(v, np.float32).reshape(-1)
    return np.ascontiguousarray(v.reshape(-1, n).T)


def prepare_inputs(inputs):
    x = np.asarray(inputs["L0_attn_hidden"], np.float32).reshape(S_FULL, DM)
    xT = np.ascontiguousarray(x.T)
    pad = np.zeros((DM, P + 1), np.float32)
    xTp = np.concatenate([pad, xT], axis=1)
    eye, selmask, bstrip, bb2, bs2 = _mk_consts()
    com = {
        "wkT": np.ascontiguousarray(np.asarray(inputs["L0_attn_w_key"], np.float32).T),
        "wvT": np.ascontiguousarray(np.asarray(inputs["L0_attn_w_value"], np.float32).T),
        "wrT": np.ascontiguousarray(np.asarray(inputs["L0_attn_w_receptance"], np.float32).T),
        "wgT": np.ascontiguousarray(np.asarray(inputs["L0_attn_w_gate"], np.float32).T),
        "woT": np.ascontiguousarray(np.asarray(inputs["L0_attn_w_output"], np.float32).T),
        "wgateT": np.ascontiguousarray(np.asarray(inputs["L0_ffn_w_gate"], np.float32).T),
        "wupT": np.ascontiguousarray(np.asarray(inputs["L0_ffn_w_up"], np.float32).T),
        "wdownT": np.ascontiguousarray(np.asarray(inputs["L0_ffn_w_down"], np.float32).T),
        "tmix": np.concatenate([_resh(inputs["L0_attn_time_mix_key"]),
                                _resh(inputs["L0_attn_time_mix_value"]),
                                _resh(inputs["L0_attn_time_mix_receptance"]),
                                _resh(inputs["L0_attn_time_mix_gate"])], axis=1),
        "td_r": _resh(inputs["L0_attn_time_decay"]),
        "tf_r": _resh(inputs["L0_attn_time_faaaa"]),
        "gnw_r": _resh(inputs["L0_attn_gn_weight"]),
        "gnb_r": _resh(inputs["L0_attn_gn_bias"]),
        "invg_r": _resh(inputs["L0_ffn_inv_gate"]),
        "invu_r": _resh(inputs["L0_ffn_inv_up"]),
        "invi_r": _resh(inputs["L0_ffn_inv_inter"]),
        "swg_r": _resh(inputs["L0_ffn_s_wgate"]),
        "swu_r": _resh(inputs["L0_ffn_s_wup"]),
        "swd_bc_in": np.ascontiguousarray(
            np.broadcast_to(np.asarray(inputs["L0_ffn_s_wdown"], np.float32)[None, :],
                            (128, DM))),
        "eye_in": eye,
        "selm_in": selmask.astype(np.uint8),
        "bstrip_in": bstrip,
        "bb2_in": bb2,
        "bs2_in": bs2,
    }
    in_maps = []
    for c in range(NCORES):
        m = dict(com)
        c0 = c * T
        m["xin"] = np.ascontiguousarray(xTp[:, c0:c0 + TH + 1])
        in_maps.append(m)
    return in_maps


def kernel(**inputs):
    if "nc" not in _CACHE:
        _CACHE["nc"] = build_program()
    nc = _CACHE["nc"]
    in_maps = prepare_inputs(inputs)
    res = run_bass_kernel_spmd(nc, in_maps, core_ids=list(range(NCORES)))
    outs = [np.asarray(r["out"], np.float32) for r in res.results]
    return np.concatenate(outs, axis=0).reshape(1, S_FULL, DM)
